# version 12
# baseline (speedup 1.0000x reference)
import contextlib
import numpy as np
import ml_dtypes
import concourse.bass as bass
import concourse.mybir as mybir
from concourse.bass_utils import run_bass_kernel_spmd

F32 = mybir.dt.float32
BF16 = mybir.dt.bfloat16
I32 = mybir.dt.int32
AF = mybir.ActivationFunctionType
OP = mybir.AluOpType
AX = mybir.AxisListType.X

C = 1024
CB = C // 128
DI = 2816
IB = DI // 128
NQ = [512, 512, 512, 512, 512, 256]
EPS = 1e-5


def build_core_program(TH, MCH, SUB, dbg=False):
    NMC = TH // MCH
    NSUB = MCH // SUB
    STT = SUB // 128
    n4 = MCH // 512

    nc = bass.Bass("TRN2", target_bir_lowering=False)

    xT_own = nc.dram_tensor("xT_own", [C, TH], F32, kind="ExternalInput").ap()
    xT_pre = nc.dram_tensor("xT_pre", [C, TH], F32, kind="ExternalInput").ap()
    x_own = nc.dram_tensor("x_own", [TH, C], F32, kind="ExternalInput").ap()
    wrvkT = nc.dram_tensor("wrvkT", [C, 3 * C], F32, kind="ExternalInput").ap()
    woutT = nc.dram_tensor("woutT", [C, C], F32, kind="ExternalInput").ap()
    wg_in = nc.dram_tensor("wg", [DI, C], F32, kind="ExternalInput").ap()
    wu_in = nc.dram_tensor("wu", [DI, C], F32, kind="ExternalInput").ap()
    wd_in = nc.dram_tensor("wd", [C, DI], F32, kind="ExternalInput").ap()
    td_in = nc.dram_tensor("td", [128, CB], F32, kind="ExternalInput").ap()
    tf_in = nc.dram_tensor("tf", [128, CB], F32, kind="ExternalInput").ap()
    mask_in = nc.dram_tensor("maskv", [128, 1], F32, kind="ExternalInput").ap()
    id_in = nc.dram_tensor("ident", [128, 128], BF16, kind="ExternalInput").ap()
    out_d = nc.dram_tensor("out", [TH, C], F32, kind="ExternalOutput").ap()
    dbg_d = {}
    if dbg:
        for nm, sh, dt in [("d_ek", [128, MCH], F32), ("d_vv", [128, MCH], F32),
                           ("d_A", [128, MCH + 1], F32), ("d_rwkv", [128, CB, MCH], BF16),
                           ("d_x1t", [128, C], F32), ("d_x1n", [128, C], F32),
                           ("d_xq", [128, C], BF16), ("d_h", [128, DI], F32),
                           ("d_hq", [128, DI], BF16), ("d_carA", [128, CB], F32),
                           ("d_wqg", [128, DI], BF16), ("d_stat", [128, 16], F32)]:
            dbg_d[nm] = nc.dram_tensor(nm, sh, dt, kind="ExternalOutput").ap()
    wqgT_d = nc.dram_tensor("wqgT", [C, DI], BF16, kind="Internal").ap()
    wquT_d = nc.dram_tensor("wquT", [C, DI], BF16, kind="Internal").ap()
    wqdT_d = nc.dram_tensor("wqdT", [DI, C], BF16, kind="Internal").ap()

    dma_n = [0]
    out_n = [0]

    uid = [0]

    def uname(n):
        uid[0] += 1
        return f"{n}_{uid[0]}"

    st = contextlib.ExitStack()
    with st:
        dma_sem = st.enter_context(nc.semaphore("dma"))
        dma_gsem = st.enter_context(nc.semaphore("dmag"))
        dma_gn = [0]
        out_sem = st.enter_context(nc.semaphore("outs"))

        def sbt(name, shape, dt):
            return st.enter_context(nc.sbuf_tensor(uname(name), shape, dt))

        ident = sbt("identity", [128, 128], BF16)
        cvec = sbt("cvec", [128, CB], F32)
        euvec = sbt("euvec", [128, CB], F32)
        maskv = sbt("maskv_sb", [128, 1], F32)
        consts = sbt("consts", [128, 8], F32)
        carA = sbt("carA", [128, CB], F32)
        carB = sbt("carB", [128, CB], F32)
        preA = sbt("preA", [128, CB], F32)
        preB = sbt("preB", [128, CB], F32)
        rwkv = sbt("rwkv", [128, CB, MCH], BF16)
        stat = sbt("stat", [128, 32], F32)

        ps1 = st.enter_context(nc.psum_tensor("ps1", [128, 1024], F32))
        ps2 = st.enter_context(nc.psum_tensor("ps2", [128, 1024], F32))
        ps3 = st.enter_context(nc.psum_tensor("ps3", [128, 1024], F32))
        psT = st.enter_context(nc.psum_tensor("psT", [128, 1024], BF16))
        psTb = psT[:]

        def tap(name, src_ap):
            if not dbg:
                return
            with nc.Block() as blk:
                @blk.sync
                def _(sync):
                    sync.dma_start(out=dbg_d[name], in_=src_ap).then_inc(dma_sem, 16)
                    dma_n[0] += 16
                    sync.wait_ge(dma_sem, dma_n[0])

        def dma(q, out, in_, wait=True, gq=False):
            sem, cnt = (dma_gsem, dma_gn) if gq else (dma_sem, dma_n)
            q.dma_start(out=out, in_=in_).then_inc(sem, 16)
            cnt[0] += 16
            if wait:
                q.wait_ge(sem, cnt[0])
            return cnt[0]

        def wait_block(cnt):
            with nc.Block() as blk:
                @blk.sync
                def _(sync):
                    sync.wait_ge(dma_sem, cnt)

        def quant_blocks(src_ap, F, out_ap):
            G = F // 16
            src3 = src_ap.rearrange("p (g k) -> p g k", k=16)
            with nc.Block() as blk:
                @blk.vector
                def _(v):
                    v.tensor_reduce(qgm[:, :G], src3, AX, OP.max, apply_absolute_value=True)
                    v.tensor_scalar(qsc[:, :G], qgm[:, :G], 1.0 / 6.0, 1e-12, OP.mult, OP.max)
                    v.reciprocal(qrs[:, :G], qsc[:, :G])
                    v.tensor_copy(qscb[:, :G], qsc[:, :G])
                @blk.scalar
                def _(s):
                    s.activation(qsg[:, :F], src_ap, AF.Sign)
                    s.activation(qa[:, :F], src_ap, AF.Abs)
            with nc.Block() as blk:
                @blk.vector
                def _(v):
                    rsb = qrs[:, :G].unsqueeze(2).broadcast_to([128, G, 16])
                    v.tensor_tensor(qa[:, :F].rearrange("p (g k) -> p g k", k=16),
                                    qa[:, :F].rearrange("p (g k) -> p g k", k=16),
                                    rsb, OP.mult)
            with nc.Block() as blk:
                @blk.scalar
                def _(s):
                    s.activation(qt1[:, :F], qa[:, :F], AF.Identity, bias=consts[:, 0:1], scale=0.5)
                    s.activation(qt2[:, :F], qa[:, :F], AF.Identity, bias=consts[:, 1:2], scale=0.25)
            with nc.Block() as blk:
                @blk.vector
                def _(v):
                    v.tensor_tensor(qt1[:, :F], qt1[:, :F], qt2[:, :F], OP.min)
                    v.tensor_tensor(qt1[:, :F], qa[:, :F], qt1[:, :F], OP.min)
                    v.tensor_scalar(qiz[:, :F], qt1[:, :F], 2.0, None, OP.mult)
                    v.tensor_scalar(qrb[:, :F], qiz[:, :F], 0.5, None, OP.mult)
            with nc.Block() as blk:
                @blk.scalar
                def _(s):
                    s.activation(qr3[:, :F], qrb[:, :F], AF.Relu, bias=consts[:, 2:3])
                    s.activation(qr4[:, :F], qrb[:, :F], AF.Relu, bias=consts[:, 3:4])
            with nc.Block() as blk:
                @blk.vector
                def _(v):
                    v.scalar_tensor_tensor(qr4[:, :F], qr4[:, :F], 2.0, qr3[:, :F], OP.mult, OP.add)
                    v.tensor_tensor(qrb[:, :F], qr4[:, :F], qrb[:, :F], OP.add)
                    v.tensor_tensor(qrb[:, :F], qrb[:, :F], qsg[:, :F], OP.mult)
                    scb = qscb[:, :G].unsqueeze(2).broadcast_to([128, G, 16])
                    v.tensor_tensor(out_ap.rearrange("p (g k) -> p g k", k=16),
                                    qrb[:, :F].rearrange("p (g k) -> p g k", k=16),
                                    scb, OP.mult)

        def transpose_evac(src_bf_ap, ntile, dst3_ap):
            for base in range(0, ntile, 8):
                nb = min(8, ntile - base)
                with nc.Block() as blk:
                    @blk.tensor
                    def _(t, base=base, nb=nb):
                        for k in range(nb):
                            t.transpose(psTb[:, k * 128:(k + 1) * 128],
                                        src_bf_ap[:, (base + k) * 128:(base + k + 1) * 128],
                                        ident[:])
                with nc.Block() as blk:
                    @blk.scalar
                    def _(s, base=base, nb=nb):
                        s.copy(dst3_ap[:, base:base + nb, :],
                               psTb[:, :nb * 128].rearrange("p (a q) -> p a q", q=128))

        esem = {n: st.enter_context(nc.semaphore(uname("e_" + n)))
                for n in ("pe", "v", "s", "g", "sy")}
        ecnt = {n: 0 for n in esem}
        slot_sems = {}

        def slot_sem(name):
            if name not in slot_sems:
                slot_sems[name] = [st.enter_context(nc.semaphore(uname("ds_" + name))), 0]
            return slot_sems[name]

        class SB:
            EMAP = {"tensor": "pe", "vector": "v", "scalar": "s",
                    "gpsimd": "g", "sync": "sy"}

            def __init__(self):
                self.ops = {k: [] for k in self.EMAP}

            def add(self, eng, fn, waits=(), inc=True, dma=False):
                tok = None
                if dma:
                    assert isinstance(dma, str), "dma= must name a slot"
                    ss = slot_sem(dma)
                    ss[1] += 16
                    tok = (ss[0], ss[1])
                elif inc:
                    nm = self.EMAP[eng]
                    ecnt[nm] += 1
                    tok = (esem[nm], ecnt[nm])
                self.ops[eng].append((tuple(waits), fn, tok, dma))
                return tok

            def emit(self):
                with nc.Block() as blk:
                    def runner(entries):
                        def _(engobj):
                            for waits, fn, tok, isdma in entries:
                                for (sh, val) in waits:
                                    engobj.wait_ge(sh, val)
                                last = fn(engobj)
                                if tok is not None and last is not None:
                                    if isdma:
                                        last.then_inc(tok[0], 16)
                                    else:
                                        last.then_inc(tok[0], 1)
                        return _
                    for eng in ("sync", "tensor", "scalar", "vector", "gpsimd"):
                        if self.ops[eng]:
                            getattr(blk, eng)(runner(self.ops[eng]))

        def q_stream(B, src_ap, F, sc_, waits_v=(), waits_s=(), t1_extra=(), nsplit=1):
            assert F % (16 * nsplit) == 0
            Fh = F // nsplit
            toks = {}
            rng = [(h * Fh, Fh) for h in range(nsplit)]
            for h, (o, L) in enumerate(rng):
                G = L // 16
                go = o // 16
                sl = src_ap[:, o:o + L]
                s3 = sl.rearrange("p (g k) -> p g k", k=16)
                v1 = B.add("vector", lambda v, s3=s3, go=go, G=G: v.tensor_reduce(
                    sc_["qgm"][:, go:go + G], s3, AX, OP.max, apply_absolute_value=True),
                    waits=list(waits_v) if h == 0 else [])
                v2 = B.add("vector", lambda v, go=go, G=G: v.tensor_scalar(
                    sc_["qsc"][:, go:go + G], sc_["qgm"][:, go:go + G], 1.0 / 6.0, 1e-12,
                    OP.mult, OP.max), waits=[v1])
                v3 = B.add("vector", lambda v, go=go, G=G: v.reciprocal(
                    sc_["qrs"][:, go:go + G], sc_["qsc"][:, go:go + G]), waits=[v2])
                v4 = B.add("vector", lambda v, go=go, G=G: v.tensor_copy(
                    sc_["qscb"][:, go:go + G], sc_["qsc"][:, go:go + G]), waits=[v3])
                sg = B.add("scalar", lambda s, sl=sl, o=o, L=L: s.activation(
                    sc_["qsg"][:, o:o + L], sl, AF.Sign), waits=list(waits_s) if h == 0 else [])
                ab = B.add("scalar", lambda s, sl=sl, o=o, L=L: s.activation(
                    sc_["qa"][:, o:o + L], sl, AF.Abs), waits=[])
                toks[h] = dict(v3=v3, v4=v4, sg=sg, ab=ab)
            s1l = None
            for h, (o, L) in enumerate(rng):
                G = L // 16
                go = o // 16
                t = toks[h]
                am = B.add("vector", lambda v, o=o, L=L, go=go, G=G: v.tensor_tensor(
                    sc_["qa"][:, o:o + L].rearrange("p (g k) -> p g k", k=16),
                    sc_["qa"][:, o:o + L].rearrange("p (g k) -> p g k", k=16),
                    sc_["qrs"][:, go:go + G].unsqueeze(2).broadcast_to([128, G, 16]), OP.mult),
                    waits=[t["ab"], t["v3"]])
                s1 = B.add("scalar", lambda s, o=o, L=L: s.activation(
                    sc_["qt1"][:, o:o + L], sc_["qa"][:, o:o + L], AF.Identity,
                    bias=consts[:, 0:1], scale=0.5), waits=[am] + (list(t1_extra) if h == 0 else []))
                s2 = B.add("scalar", lambda s, o=o, L=L: s.activation(
                    sc_["qt2"][:, o:o + L], sc_["qa"][:, o:o + L], AF.Identity,
                    bias=consts[:, 1:2], scale=0.25), waits=[])
                m1 = B.add("vector", lambda v, o=o, L=L: v.tensor_tensor(
                    sc_["qt1"][:, o:o + L], sc_["qt1"][:, o:o + L], sc_["qt2"][:, o:o + L],
                    OP.min), waits=[s2])
                m2 = B.add("vector", lambda v, o=o, L=L: v.tensor_tensor(
                    sc_["qt1"][:, o:o + L], sc_["qa"][:, o:o + L], sc_["qt1"][:, o:o + L],
                    OP.min), waits=[])
                i1 = B.add("vector", lambda v, o=o, L=L: v.tensor_scalar(
                    sc_["qiz"][:, o:o + L], sc_["qt1"][:, o:o + L], 2.0, None, OP.mult), waits=[])
                r1 = B.add("vector", lambda v, o=o, L=L: v.tensor_scalar(
                    sc_["qrb"][:, o:o + L], sc_["qiz"][:, o:o + L], 0.5, None, OP.mult), waits=[])
                r3 = B.add("scalar", lambda s, o=o, L=L: s.activation(
                    sc_["qr3"][:, o:o + L], sc_["qrb"][:, o:o + L], AF.Relu,
                    bias=consts[:, 2:3]), waits=[r1])
                r4 = B.add("scalar", lambda s, o=o, L=L: s.activation(
                    sc_["qr4"][:, o:o + L], sc_["qrb"][:, o:o + L], AF.Relu,
                    bias=consts[:, 3:4]), waits=[])
                toks[h].update(r4=r4, i1=i1)
                s1l = s1
            out_tok = None
            for h, (o, L) in enumerate(rng):
                G = L // 16
                go = o // 16
                t = toks[h]
                u1 = B.add("vector", lambda v, o=o, L=L: v.scalar_tensor_tensor(
                    sc_["qr4"][:, o:o + L], sc_["qr4"][:, o:o + L], 2.0,
                    sc_["qr3"][:, o:o + L], OP.mult, OP.add), waits=[t["r4"]])
                u2 = B.add("vector", lambda v, o=o, L=L: v.tensor_tensor(
                    sc_["qrb"][:, o:o + L], sc_["qr4"][:, o:o + L], sc_["qrb"][:, o:o + L],
                    OP.add), waits=[])
                u3 = B.add("vector", lambda v, o=o, L=L: v.tensor_tensor(
                    sc_["qrb"][:, o:o + L], sc_["qrb"][:, o:o + L], sc_["qsg"][:, o:o + L],
                    OP.mult), waits=[t["sg"]])
                out_tok = B.add("vector", lambda v, o=o, L=L, go=go, G=G: v.tensor_tensor(
                    sc_["qout"][:, o:o + L].rearrange("p (g k) -> p g k", k=16),
                    sc_["qrb"][:, o:o + L].rearrange("p (g k) -> p g k", k=16),
                    sc_["qscb"][:, go:go + G].unsqueeze(2).broadcast_to([128, G, 16]), OP.mult),
                    waits=[t["v4"]])
            return out_tok, toks[nsplit - 1]["i1"], s1l

        def ln_stream(B, x_ap, out_ap, dump_ap, waits=(), so=0):
            c = lambda i: stat[:, so + i:so + i + 1]
            vm = B.add("vector", lambda v: v.tensor_reduce(c(0), x_ap, AX, OP.add),
                       waits=list(waits))
            sq = B.add("scalar", lambda s: s.activation(dump_ap, x_ap, AF.Square,
                                                        accum_out=c(1)),
                       waits=list(waits))
            t1 = B.add("vector", lambda v: v.tensor_scalar(c(2), c(0), 1.0 / C, None, OP.mult),
                       waits=[vm])
            t2 = B.add("vector", lambda v: v.tensor_tensor(c(3), c(2), c(2), OP.mult), waits=[t1])
            t3 = B.add("vector", lambda v: v.tensor_scalar(c(4), c(1), 1.0 / C, EPS,
                                                           OP.mult, OP.add), waits=[sq])
            t4 = B.add("vector", lambda v: v.tensor_tensor(c(5), c(4), c(3), OP.subtract),
                       waits=[t2, t3])
            s1 = B.add("scalar", lambda s: s.activation(c(6), c(5), AF.Sqrt), waits=[t4])
            t5 = B.add("vector", lambda v: v.reciprocal(c(7), c(6)), waits=[s1])
            t6 = B.add("vector", lambda v: v.tensor_tensor(c(3), c(2), c(7), OP.mult), waits=[t5])
            t7 = B.add("vector", lambda v: v.tensor_scalar(c(5), c(3), -1.0, None, OP.mult),
                       waits=[t6])
            return B.add("scalar", lambda s: s.activation(out_ap, x_ap, AF.Identity,
                                                          bias=c(5), scale=c(7)),
                         waits=[t7])

        def ln_blocks(x_ap, out_ap, dump_ap):
            with nc.Block() as blk:
                @blk.vector
                def _(v):
                    v.tensor_reduce(stat[:, 8:9], x_ap, AX, OP.add)
            with nc.Block() as blk:
                @blk.vector
                def _(v):
                    v.tensor_scalar(stat[:, 0:1], stat[:, 8:9], 1.0 / C, None, OP.mult)
            with nc.Block() as blk:
                @blk.scalar
                def _(s):
                    s.activation(dump_ap, x_ap, AF.Square,
                                 bias=stat[:, 0:1], scale=-1.0, accum_out=stat[:, 9:10])
            with nc.Block() as blk:
                @blk.vector
                def _(v):
                    v.tensor_scalar(stat[:, 1:2], stat[:, 9:10], 1.0 / C, EPS, OP.mult, OP.add)
            with nc.Block() as blk:
                @blk.scalar
                def _(s):
                    s.activation(stat[:, 2:3], stat[:, 1:2], AF.Sqrt)
            with nc.Block() as blk:
                @blk.vector
                def _(v):
                    v.reciprocal(stat[:, 3:4], stat[:, 2:3])
            with nc.Block() as blk:
                @blk.vector
                def _(v):
                    v.tensor_tensor(stat[:, 10:11], stat[:, 0:1], stat[:, 3:4], OP.mult)
            with nc.Block() as blk:
                @blk.vector
                def _(v):
                    v.tensor_scalar(stat[:, 4:5], stat[:, 10:11], -1.0, None, OP.mult)
            with nc.Block() as blk:
                @blk.scalar
                def _(s):
                    s.activation(out_ap, x_ap, AF.Identity,
                                 bias=stat[:, 4:5], scale=stat[:, 3:4])

        with nc.Block() as blk:
            @blk.sync
            def _(sync):
                dma(sync, ident[:], id_in, wait=False)
                dma(sync, maskv[:], mask_in, wait=False)
                dma(sync, cvec[:], td_in, wait=False)
                dma(sync, euvec[:], tf_in, wait=True)

            @blk.gpsimd
            def _(g):
                g.memset(consts[:, 0:1], 1.0)
                g.memset(consts[:, 1:2], 2.0)
                g.memset(consts[:, 2:3], -2.0)
                g.memset(consts[:, 3:4], -3.0)
                g.memset(carA[:], 0.0)
                g.memset(carB[:], 0.0)
                g.memset(preA[:], 0.0)
                g.memset(preB[:], 0.0)
        with nc.Block() as blk:
            @blk.scalar
            def _(s):
                s.activation(stat[:, 0:CB], cvec[:], AF.Exp)
        with nc.Block() as blk:
            @blk.scalar
            def _(s):
                s.activation(cvec[:], stat[:, 0:CB], AF.Exp, scale=-1.0)
        with nc.Block() as blk:
            @blk.scalar
            def _(s):
                s.activation(euvec[:], euvec[:], AF.Exp)

        with nc.named_scope("wquant"), contextlib.ExitStack() as parena:
            pn = lambda n, sh, dt: parena.enter_context(nc.sbuf_tensor(uname(n), sh, dt))
            wstage = pn("wstage", [128, 2, DI], F32)
            wtr = pn("wtr", [128, 2, IB, 128], BF16)
            wsc = []
            for p in range(2):
                d = {"qa": pn("wqa", [128, DI], F32), "qt1": pn("wqt1", [128, DI], F32),
                     "qt2": pn("wqt2", [128, DI], F32), "qrb": pn("wqrb", [128, DI], BF16),
                     "qr3": pn("wqr3", [128, DI], BF16), "qr4": pn("wqr4", [128, DI], BF16),
                     "qsg": pn("wqsg", [128, DI], BF16), "qgm": pn("wqgm", [128, DI // 16], F32),
                     "qsc": pn("wqsc", [128, DI // 16], F32), "qrs": pn("wqrs", [128, DI // 16], F32),
                     "qscb": pn("wqscb", [128, DI // 16], BF16)}
                d["qiz"] = d["qt2"][:].bitcast(I32)
                d["qout"] = d["qa"][:].bitcast(BF16)[:, :DI]
                wsc.append(d)
            wtiles = ([(wg_in, wqgT_d, 2 * i, 2, 2 * C) for i in range(IB // 2)] +
                      [(wg_in, wqgT_d, IB - 1, 1, C)] +
                      [(wu_in, wquT_d, 2 * i, 2, 2 * C) for i in range(IB // 2)] +
                      [(wu_in, wquT_d, IB - 1, 1, C)] +
                      [(wd_in, wqdT_d, i, 1, DI) for i in range(CB)])
            B = SB()
            load_toks, sa_toks, tr_toks, ev_toks = {}, {}, {}, {}
            def wq_load(wi):
                srcw, dst, row0, nrows, F = wtiles[wi]
                par = wi % 2
                w = []
                if wi >= 2:
                    w = [sa_toks[wi - 2]]
                load_toks[wi] = B.add("sync", lambda sy, par=par, srcw=srcw, row0=row0, nrows=nrows, F=F:
                                      sy.dma_start(out=wstage[:, par, :F].rearrange("p (a c) -> p a c", a=nrows),
                                                   in_=srcw[row0 * 128:(row0 + nrows) * 128, :].rearrange(
                                                       "(a p) c -> p a c", p=128)),
                                      waits=w, dma=f"wst{par}")
            wq_load(0)
            wq_load(1)
            prev_ev = None
            for wi, (srcw, dst, row0, nrows, F) in enumerate(wtiles):
                par = wi % 2
                sc_ = wsc[par]
                ncols = F // 128
                vo, i1, s1 = q_stream(B, wstage[:, par, :F],
                                      F, sc_, waits_v=[load_toks[wi]], waits_s=[load_toks[wi]],
                                      t1_extra=(), nsplit=2)
                sa_toks[wi] = s1
                pe_w = [vo]
                if wi >= 2 and (wi - 2) in tr_toks:
                    pass
                evs = []
                for bi, base in enumerate(range(0, ncols, 8)):
                    nb = min(8, ncols - base)
                    tw = list(pe_w) if bi == 0 else []
                    if prev_ev is not None:
                        tw.append(prev_ev)
                    if bi == 0 and wi - 2 in ev_toks:
                        tw.append(ev_toks[wi - 2])
                    ptb = B.add("tensor", lambda t, sc_=sc_, base=base, nb=nb: (
                        [t.transpose(psTb[:, k * 128:(k + 1) * 128],
                                     sc_["qout"][:, (base + k) * 128:(base + k + 1) * 128], ident[:])
                         for k in range(nb)])[-1], waits=tw)
                    prev_ev = B.add("scalar", lambda s, par=par, base=base, nb=nb: s.copy(
                        wtr[:, par, base:base + nb, :],
                        psTb[:, :nb * 128].rearrange("p (a q) -> p a q", q=128)), waits=[ptb])
                    evs.append(prev_ev)
                if wi + 2 < len(wtiles):
                    wq_load(wi + 2)
                cpf = ncols // nrows
                st_w = [evs[-1]]
                for a in range(nrows):
                    ev_toks[wi] = B.add("sync", lambda sy, dst=dst, row0=row0, a=a, par=par, cpf=cpf: sy.dma_start(
                        out=dst.rearrange("(cb p) i -> p cb i", p=128)[:, :, (row0 + a) * 128:(row0 + a + 1) * 128],
                        in_=wtr[:, par, a * cpf:(a + 1) * cpf, :]), waits=st_w if a == 0 else [], dma=f"wqst{par}")
            def waitall(sy):
                for nm in ("wqst0", "wqst1"):
                    if nm in slot_sems:
                        sy.wait_ge(slot_sems[nm][0], slot_sems[nm][1])
                return None
            B.add("sync", waitall, inc=False)
            B.emit()

        qa = sbt("qa", [128, DI], F32)
        qt1 = sbt("qt1", [128, DI], F32)
        qt2 = sbt("qt2", [128, DI], F32)
        qiz = qt2[:].bitcast(I32)
        qrb = sbt("qrb", [128, DI], BF16)
        qr3 = sbt("qr3", [128, DI], BF16)
        qr4 = sbt("qr4", [128, DI], BF16)
        qsg = sbt("qsg", [128, DI], BF16)
        qout = qa[:].bitcast(BF16)[:, :DI]
        qgm = sbt("qgm", [128, DI // 16], F32)
        qsc = sbt("qsc", [128, DI // 16], F32)
        qrs = sbt("qrs", [128, DI // 16], F32)
        qscb = sbt("qscb", [128, DI // 16], BF16)

        def emit_mixer_chunk(ci, prefix, bufs, tap_this=False):
            (wrvk_str, xt_bf, ekb, vvb, ssb, Abuf, Bbuf, numb, ctile, thb) = bufs
            xsrc = xT_pre if prefix else xT_own
            t0 = ci * MCH
            B = SB()
            dxt = B.add("gpsimd", lambda g: g.dma_start(
                out=xt_bf[:], in_=xsrc.rearrange("(cb p) t -> p cb t", p=128)[:, :, t0:t0 + MCH]),
                dma="xt")
            wload, pe_dt, sev, vch = {}, {}, {}, {}
            orders = {}
            for j in range(CB):
                orders[j] = ([(CB + j, ps1), (2 * CB + j, ps2)] if prefix
                             else [(j, ps3), (CB + j, ps1), (2 * CB + j, ps2)])
            def wrvk_load(j):
                par = j % 2
                w = [pe_dt[(j - 2, len(orders[j - 2]) - 1)]] if j >= 2 else []
                for oi, (dt, _ps) in enumerate(orders[j]):
                    wload[(j, oi)] = B.add("gpsimd", lambda g, par=par, oi=oi, dt=dt: g.dma_start(
                        out=wrvk_str[:, par, oi, :, :],
                        in_=wrvkT.rearrange("(cb p) d -> p cb d", p=128)[:, :, dt * 128:(dt + 1) * 128]),
                        waits=w if oi == 0 else [], dma=f"wrvk{par}")
                wload[j] = wload[(j, len(orders[j]) - 1)]
            wrvk_load(0)
            wrvk_load(1)
            for j in range(CB):
                par = j % 2
                order = orders[j]
                for oi, (dt, ps) in enumerate(order):
                    w = [wload[j]]
                    if oi == 0:
                        w.append(dxt)
                    if j >= 1 and (j - 1, oi) in sev:
                        w.append(sev[(j - 1, oi)])
                    def mmj(t, ps=ps, oi=oi, par=par):
                        for tt in range(n4):
                            for cb in range(CB):
                                last = t.matmul(ps[:, tt * 512:(tt + 1) * 512],
                                                wrvk_str[:, par, oi, cb, :],
                                                xt_bf[:, cb, tt * 512:(tt + 1) * 512],
                                                start=(cb == 0), stop=(cb == CB - 1))
                        return last
                    pe_dt[(j, oi)] = B.add("tensor", mmj, waits=w)
                if j + 2 < CB:
                    wrvk_load(j + 2)
                evw2 = [vch[j - 2]] if j >= 2 else []
                if prefix:
                    sev[(j, 0)] = B.add("scalar", lambda s, par=par: s.copy(vvb[:, par, :], ps1[:, :MCH]),
                                        waits=[pe_dt[(j, 0)]] + evw2)
                    sev[(j, 1)] = B.add("scalar", lambda s, par=par: s.activation(
                        ekb[:, par, :], ps2[:, :MCH], AF.Exp), waits=[pe_dt[(j, 1)]])
                else:
                    sev[(j, 0)] = B.add("scalar", lambda s, par=par: s.activation(
                        thb[:, par, :], ps3[:, :MCH], AF.Tanh, scale=0.5),
                        waits=[pe_dt[(j, 0)]] + evw2)
                    sev[(j, 1)] = B.add("scalar", lambda s, par=par: s.copy(vvb[:, par, :], ps1[:, :MCH]),
                                        waits=[pe_dt[(j, 1)]])
                    sev[(j, 2)] = B.add("scalar", lambda s, par=par: s.activation(
                        ekb[:, par, :], ps2[:, :MCH], AF.Exp), waits=[pe_dt[(j, 2)]])
                def chain(v, j=j, par=par, prefix=prefix):
                    v.tensor_scalar(ctile[:], ekb[:, par, :], 0.0, cvec[:, j:j + 1], OP.mult, OP.add)
                    v.tensor_tensor(ssb[:], ekb[:, par, :], vvb[:, par, :], OP.mult)
                    if prefix:
                        v.tensor_tensor_scan(Abuf[:, :MCH], ctile[:], ssb[:],
                                             preA[:, j:j + 1], OP.mult, OP.add)
                        v.tensor_tensor_scan(Bbuf[:, :MCH], ctile[:], ekb[:, par, :],
                                             preB[:, j:j + 1], OP.mult, OP.add)
                        v.tensor_copy(preA[:, j:j + 1], Abuf[:, MCH - 1:MCH])
                        return v.tensor_copy(preB[:, j:j + 1], Bbuf[:, MCH - 1:MCH])
                    v.tensor_copy(Abuf[:, 0:1], carA[:, j:j + 1])
                    v.tensor_copy(Bbuf[:, 0:1], carB[:, j:j + 1])
                    v.tensor_tensor_scan(Abuf[:, 1:MCH + 1], ctile[:], ssb[:],
                                         carA[:, j:j + 1], OP.mult, OP.add)
                    v.tensor_tensor_scan(Bbuf[:, 1:MCH + 1], ctile[:], ekb[:, par, :],
                                         carB[:, j:j + 1], OP.mult, OP.add)
                    v.tensor_copy(carA[:, j:j + 1], Abuf[:, MCH:MCH + 1])
                    v.tensor_copy(carB[:, j:j + 1], Bbuf[:, MCH:MCH + 1])
                    v.scalar_tensor_tensor(numb[:], ssb[:], euvec[:, j:j + 1],
                                           Abuf[:, 0:MCH], OP.mult, OP.add)
                    v.scalar_tensor_tensor(ssb[:], ekb[:, par, :], euvec[:, j:j + 1],
                                           Bbuf[:, 0:MCH], OP.mult, OP.add)
                    v.reciprocal(ctile[:], ssb[:])
                    v.tensor_tensor(numb[:], numb[:], ctile[:], OP.mult)
                    v.tensor_scalar(thb[:, par, :], thb[:, par, :], 0.5, 0.5, OP.mult, OP.add)
                    return v.tensor_tensor(rwkv[:, j, :], numb[:], thb[:, par, :], OP.mult)
                vch[j] = B.add("vector", chain, waits=[sev[(j, len(order) - 1)]])
            B.emit()

        def emit_post_sub(ci, sc, bufs, tap_this=False):
            (wout_bf, x1n, x1t, xst, xqT, wgu, hbuf, hqT, wdT, x2t, outst) = bufs
            assert STT == 2
            t0 = ci * MCH
            B = SB()
            x1ta, x1tb = qt1[:, 0:C], qt1[:, C:2 * C]
            psc = {"qa": qa[:], "qt1": qt1[:], "qt2": qt2[:], "qiz": qiz, "qrb": qrb[:],
                   "qr3": qr3[:], "qr4": qr4[:], "qsg": qsg[:], "qgm": qgm[:],
                   "qsc": qsc[:], "qrs": qrs[:], "qscb": qscb[:], "qout": qout}

            dxx = None
            for tt in range(STT):
                dxx = B.add("sync", lambda sy, tt=tt, gt=t0 + (sc * STT + tt) * 128:
                            sy.dma_start(out=xst[:, tt, :], in_=x_own[gt:gt + 128, :]), dma="x")
            dwd0 = B.add("sync", lambda sy: sy.dma_start(
                out=wdT[:], in_=wqdT_d.rearrange("(ib p) d -> p ib d", p=128)[:, :, 0:512]), dma="wd0")
            dgu = {}
            def gu_load(par, ni, waits=()):
                o, n = sum(NQ[:ni]), NQ[ni]
                B.add("sync", lambda sy, par=par, o=o, n=n: sy.dma_start(
                    out=wgu[:, par, 0, :, :n],
                    in_=wqgT_d.rearrange("(cb p) i -> p cb i", p=128)[:, :, o:o + n]),
                    waits=waits, dma=f"gu{par}")
                dgu[ni] = B.add("sync", lambda sy, par=par, o=o, n=n: sy.dma_start(
                    out=wgu[:, par, 1, :, :n],
                    in_=wquT_d.rearrange("(cb p) i -> p cb i", p=128)[:, :, o:o + n]),
                    dma=f"gu{par}")
            gu_load(0, 0)
            gu_load(1, 1)

            def mm2(t, lt, ps):
                for half in range(2):
                    for j in range(CB):
                        last = t.matmul(ps[:, half * 512:(half + 1) * 512],
                                        rwkv[:, j, lt * 128:(lt + 1) * 128],
                                        wout_bf[:, j, half * 512:(half + 1) * 512],
                                        start=(j == 0), stop=(j == CB - 1))
                return last
            pmm2a = B.add("tensor", lambda t: mm2(t, sc * STT + 0, ps1), waits=[dxx])
            pmm2b = B.add("tensor", lambda t: mm2(t, sc * STT + 1, ps2), waits=[])
            vra = B.add("vector", lambda v: v.tensor_tensor(x1ta, ps1[:, :C], xst[:, 0, :], OP.add),
                        waits=[pmm2a])
            vrb = B.add("vector", lambda v: v.tensor_tensor(x1tb, ps2[:, :C], xst[:, 1, :], OP.add),
                        waits=[pmm2b])
            ln0 = ln_stream(B, x1ta, x1n[:, 0, :], qt2[:, 0:C], waits=[vra], so=16)
            ln1t = ln_stream(B, x1tb, x1n[:, 1, :], qt2[:, C:2 * C], waits=[vrb], so=24)

            xf = x1n[:].rearrange("p a c -> p (a c)")
            xqv, xq_i1, xq_s1 = q_stream(B, xf, 2 * C, psc, waits_v=[ln1t], waits_s=[ln1t],
                                         t1_extra=[], nsplit=2)
            def trx(t, base):
                for k in range(8):
                    last = t.transpose(psTb[:, k * 128:(k + 1) * 128],
                                       qout[:, (base + k) * 128:(base + k + 1) * 128], ident[:])
                return last
            ptr0 = B.add("tensor", lambda t: trx(t, 0), waits=[xqv])
            se0 = B.add("scalar", lambda s: s.copy(xqT[:, :, 0:128],
                                                   psTb[:, :1024].rearrange("p (a q) -> p a q", q=128)),
                        waits=[ptr0])
            ptr1 = B.add("tensor", lambda t: trx(t, 8), waits=[se0])
            se1 = B.add("scalar", lambda s: s.copy(xqT[:, :, 128:256],
                                                   psTb[:, :1024].rearrange("p (a q) -> p a q", q=128)),
                        waits=[ptr1])

            hv = None
            sil = {}
            off = 0
            for ni, nw in enumerate(NQ):
                par = ni % 2
                gps = ps1 if par == 0 else ps3
                gw = [dgu[ni], se1]
                if ni >= 2:
                    gw.append(sil[ni - 2])
                if ni == 0:
                    gw += [vra, vrb]
                def gmm(t, gps=gps, par=par, nw=nw):
                    for tt in range(STT):
                        for cb in range(CB):
                            last = t.matmul(gps[:, tt * 512:tt * 512 + nw],
                                            xqT[:, cb, tt * 128:(tt + 1) * 128],
                                            wgu[:, par, 0, cb, :nw],
                                            start=(cb == 0), stop=(cb == CB - 1))
                    return last
                pg = B.add("tensor", gmm, waits=gw)
                sl = B.add("scalar", lambda s, gps=gps, nw=nw: (
                    s.activation(qt1[:, :nw], gps[:, :nw], AF.Silu),
                    s.activation(qt1[:, 512:512 + nw], gps[:, 512:512 + nw], AF.Silu))[-1],
                    waits=[pg, xq_i1] if ni == 0 else [pg])
                sil[ni] = sl
                uw = [sl]
                if hv is not None:
                    uw.append(hv)
                def umm(t, par=par, nw=nw):
                    for tt in range(STT):
                        for cb in range(CB):
                            last = t.matmul(ps2[:, tt * 512:tt * 512 + nw],
                                            xqT[:, cb, tt * 128:(tt + 1) * 128],
                                            wgu[:, par, 1, cb, :nw],
                                            start=(cb == 0), stop=(cb == CB - 1))
                    return last
                pu = B.add("tensor", umm, waits=uw)
                hv = B.add("vector", lambda v, off=off, nw=nw: (
                    v.tensor_tensor(hbuf[:, 0, off:off + nw], qt1[:, :nw], ps2[:, :nw], OP.mult),
                    v.tensor_tensor(hbuf[:, 1, off:off + nw], qt1[:, 512:512 + nw],
                                    ps2[:, 512:512 + nw], OP.mult))[-1],
                    waits=[pu])
                if ni + 2 < len(NQ):
                    gu_load(par, ni + 2, waits=[pu])
                off += nw
            wdT1 = wgu[:].rearrange("p a b cb i -> p (a b cb i)")[:, :IB * 512].rearrange(
                "p (ib i) -> p ib i", i=512)
            dwd1 = B.add("sync", lambda sy: sy.dma_start(
                out=wdT1, in_=wqdT_d.rearrange("(ib p) d -> p ib d", p=128)[:, :, 512:1024]),
                waits=[hv], dma="wd1")

            prev_evac = se1
            hq_toks = {}
            def tail(tt):
                psx = ps1 if tt == 0 else ps2
                vr2 = B.add("vector", lambda v, psx=psx, tt=tt: v.tensor_tensor(
                    xst[:, tt, :], psx[:, :C], x1n[:, tt, :], OP.add),
                    waits=[hq_toks[(tt, "pm3")], hq_toks[(tt, "s1")]])
                nmo = ln_stream(B, xst[:, tt, :], outst[:, tt % 2, :], x1n[:, tt, :],
                                waits=[vr2, (out_sem, max(0, out_n[0] - 16))],
                                so=16 if tt == 0 else 24)
                gt = t0 + (sc * STT + tt) * 128
                B.add("gpsimd", lambda g, gt=gt, tt=tt: g.dma_start(
                    out=out_d[gt:gt + 128, :], in_=outst[:, tt % 2, :]), waits=[nmo],
                    inc=False, dma=False)
                ent = B.ops["gpsimd"][-1]
                def mk(fn0):
                    def _f(g):
                        last = fn0(g)
                        last.then_inc(out_sem, 16)
                        return None
                    return _f
                B.ops["gpsimd"][-1] = (ent[0], mk(ent[1]), None, False)
                out_n[0] += 16

            prev_i1 = None
            for tt in range(STT):
                ha = hbuf[:, tt, :]
                w0 = [hv]
                hqv, h_i1, h_s1 = q_stream(B, ha, DI, psc,
                                           waits_v=w0, waits_s=w0,
                                           t1_extra=[xq_i1] if tt == 0 else [prev_i1],
                                           nsplit=2)
                prev_i1 = h_i1
                hq_toks[(tt, "s1")] = h_s1
                evs = []
                for base in range(0, IB, 8):
                    nb = min(8, IB - base)
                    ptb = B.add("tensor", lambda t, base=base, nb=nb: (
                        [t.transpose(psTb[:, k * 128:(k + 1) * 128],
                                     qout[:, (base + k) * 128:(base + k + 1) * 128], ident[:])
                         for k in range(nb)])[-1], waits=[hqv, prev_evac])
                    prev_evac = B.add("scalar", lambda s, tt=tt, base=base, nb=nb: s.copy(
                        hqT[:, tt, base:base + nb, :],
                        psTb[:, :nb * 128].rearrange("p (a q) -> p a q", q=128)), waits=[ptb])
                    evs.append(prev_evac)
                psx = ps1 if tt == 0 else ps2
                mmw = [evs[-1], dwd0, dwd1]
                if tt == 0:
                    mmw += [hv, sil[len(NQ) - 2]]
                hq_toks[(tt, "pm3")] = B.add("tensor", lambda t, psx=psx, tt=tt: (
                    [t.matmul(psx[:, half * 512:(half + 1) * 512], hqT[:, tt, ib, :],
                              (wdT if half == 0 else wdT1)[:, ib, :],
                              start=(ib == 0), stop=(ib == IB - 1))
                     for half in range(2) for ib in range(IB)][-1]), waits=mmw)
                if tt == 1:
                    tail(0)
            tail(1)
            B.emit()

        def with_mixer_arena(fn):
            with contextlib.ExitStack() as ma:
                mb = lambda name, shape, dt: ma.enter_context(nc.sbuf_tensor(uname(name), shape, dt))
                bufs = (mb("wrvk_str", [128, 2, 3, CB, 128], BF16),
                        mb("xt_bf", [128, CB, MCH], BF16),
                        mb("ekb", [128, 2, MCH], F32),
                        mb("vvb", [128, 2, MCH], F32),
                        mb("ssb", [128, MCH], F32),
                        mb("Abuf", [128, MCH + 1], F32),
                        mb("Bbuf", [128, MCH + 1], F32),
                        mb("numb", [128, MCH], F32),
                        mb("ctile", [128, MCH], F32),
                        mb("thb", [128, 2, MCH], F32))
                fn(bufs)

        with nc.named_scope("prefix"):
            for ci in range(NMC):
                with_mixer_arena(lambda bufs, ci=ci: emit_mixer_chunk(ci, True, bufs))
        with nc.Block() as blk:
            @blk.vector
            def _(v):
                v.tensor_scalar(carA[:], preA[:], maskv[:, 0:1], None, OP.mult)
                v.tensor_scalar(carB[:], preB[:], maskv[:, 0:1], None, OP.mult)
        tap("d_carA", carA[:])

        for ci in range(NMC):
            with nc.named_scope(f"mixer{ci}"):
                with_mixer_arena(lambda bufs, ci=ci: emit_mixer_chunk(ci, False, bufs, tap_this=(ci == 0)))
            if dbg and ci == 0:
                tap("d_rwkv", rwkv[:])
            with contextlib.ExitStack() as fa:
                fb = lambda name, shape, dt: fa.enter_context(nc.sbuf_tensor(uname(name), shape, dt))
                wout_bf = fb("wout_bf", [128, CB, C], BF16)
                x1n = fb("x1n", [128, STT, C], F32)
                x1t = qt1[:, 0:C]
                xst = fb("xst", [128, STT, C], F32)
                xqT = fb("xqT", [128, CB, SUB], BF16)
                wgu = fb("wgu", [128, 2, 2, CB, 512], BF16)
                hbuf = fb("hbuf", [128, STT, DI], F32)
                hqT = fb("hqT", [128, 2, IB, 128], BF16)
                wdT = fb("wdT", [128, IB, 512], BF16)
                x2t = None
                outst = fb("outst", [128, 2, C], F32)
                with nc.Block() as blk:
                    @blk.gpsimd
                    def _(g):
                        dma(g, wout_bf[:],
                            woutT.rearrange("(cb p) d -> p cb d", p=128), gq=True)
                fbufs = (wout_bf, x1n, x1t, xst, xqT, wgu, hbuf, hqT, wdT, x2t, outst)
                for sc in range(NSUB):
                    with nc.named_scope(f"post{ci}_{sc}"):
                        emit_post_sub(ci, sc, fbufs, tap_this=(ci == 0 and sc == 0))

        with nc.Block() as blk:
            @blk.gpsimd
            def _(g):
                g.wait_ge(out_sem, out_n[0])
    return nc


def kernel(L0_attn_x, L0_attn_W_rvk, L0_attn_W_out, L0_attn_time_decay,
           L0_attn_time_first, L0_ffn_w_gate, L0_ffn_w_up, L0_ffn_w_down):
    B, T, _ = L0_attn_x.shape
    TH = T // 2
    x = np.ascontiguousarray(np.asarray(L0_attn_x, np.float32))
    wrvkT = np.ascontiguousarray(np.asarray(L0_attn_W_rvk, np.float32).T)
    woutT = np.ascontiguousarray(np.asarray(L0_attn_W_out, np.float32).T)
    wg = np.ascontiguousarray(np.asarray(L0_ffn_w_gate, np.float32))
    wu = np.ascontiguousarray(np.asarray(L0_ffn_w_up, np.float32))
    wd = np.ascontiguousarray(np.asarray(L0_ffn_w_down, np.float32))
    td = np.ascontiguousarray(np.asarray(L0_attn_time_decay, np.float32))
    tf = np.ascontiguousarray(np.asarray(L0_attn_time_first, np.float32))
    ident = np.eye(128, dtype=ml_dtypes.bfloat16).view(np.uint16)

    nc = build_core_program(TH, MCH=1024, SUB=256)
    in_maps = []
    for c in range(8):
        b, h = c // 2, c % 2
        xb = x[b]
        xbT = np.ascontiguousarray(xb.T)
        own = slice(h * TH, (h + 1) * TH)
        in_maps.append({
            "xT_own": np.ascontiguousarray(xbT[:, own]),
            "xT_pre": np.ascontiguousarray(xbT[:, 0:TH]),
            "x_own": np.ascontiguousarray(xb[own]),
            "wrvkT": wrvkT, "woutT": woutT,
            "wg": wg, "wu": wu, "wd": wd,
            "td": np.ascontiguousarray(td.reshape(CB, 128).T),
            "tf": np.ascontiguousarray(tf.reshape(CB, 128).T),
            "maskv": np.full((128, 1), float(h), np.float32),
            "ident": ident,
        })
    res = run_bass_kernel_spmd(nc, in_maps, core_ids=list(range(8)))
    out = np.empty((B, T, C), np.float32)
    for c in range(8):
        b, h = c // 2, c % 2
        out[b, h * TH:(h + 1) * TH] = np.asarray(res.results[c]["out"])
    return out


# version 13 (final)
# speedup vs baseline: 1.0084x; 1.0084x over previous
import contextlib
import numpy as np
import ml_dtypes
import concourse.bass as bass
import concourse.mybir as mybir
from concourse.bass_utils import run_bass_kernel_spmd

F32 = mybir.dt.float32
BF16 = mybir.dt.bfloat16
I32 = mybir.dt.int32
AF = mybir.ActivationFunctionType
OP = mybir.AluOpType
AX = mybir.AxisListType.X

C = 1024
CB = C // 128
DI = 2816
IB = DI // 128
NQ = [512, 512, 512, 512, 512, 256]
EPS = 1e-5


def build_core_program(TH, MCH, SUB, dbg=False):
    NMC = TH // MCH
    NSUB = MCH // SUB
    STT = SUB // 128
    n4 = MCH // 512

    nc = bass.Bass("TRN2", target_bir_lowering=False)

    xT_own = nc.dram_tensor("xT_own", [C, TH], F32, kind="ExternalInput").ap()
    xT_pre = nc.dram_tensor("xT_pre", [C, TH], F32, kind="ExternalInput").ap()
    x_own = nc.dram_tensor("x_own", [TH, C], F32, kind="ExternalInput").ap()
    wrvkT = nc.dram_tensor("wrvkT", [C, 3 * C], F32, kind="ExternalInput").ap()
    woutT = nc.dram_tensor("woutT", [C, C], F32, kind="ExternalInput").ap()
    wg_in = nc.dram_tensor("wg", [DI, C], F32, kind="ExternalInput").ap()
    wu_in = nc.dram_tensor("wu", [DI, C], F32, kind="ExternalInput").ap()
    wd_in = nc.dram_tensor("wd", [C, DI], F32, kind="ExternalInput").ap()
    td_in = nc.dram_tensor("td", [128, CB], F32, kind="ExternalInput").ap()
    tf_in = nc.dram_tensor("tf", [128, CB], F32, kind="ExternalInput").ap()
    mask_in = nc.dram_tensor("maskv", [128, 1], F32, kind="ExternalInput").ap()
    id_in = nc.dram_tensor("ident", [128, 128], BF16, kind="ExternalInput").ap()
    out_d = nc.dram_tensor("out", [TH, C], F32, kind="ExternalOutput").ap()
    dbg_d = {}
    if dbg:
        for nm, sh, dt in [("d_ek", [128, MCH], F32), ("d_vv", [128, MCH], F32),
                           ("d_A", [128, MCH + 1], F32), ("d_rwkv", [128, CB, MCH], BF16),
                           ("d_x1t", [128, C], F32), ("d_x1n", [128, C], F32),
                           ("d_xq", [128, C], BF16), ("d_h", [128, DI], F32),
                           ("d_hq", [128, DI], BF16), ("d_carA", [128, CB], F32),
                           ("d_wqg", [128, DI], BF16), ("d_stat", [128, 16], F32)]:
            dbg_d[nm] = nc.dram_tensor(nm, sh, dt, kind="ExternalOutput").ap()
    wqgT_d = nc.dram_tensor("wqgT", [C, DI], BF16, kind="Internal").ap()
    wquT_d = nc.dram_tensor("wquT", [C, DI], BF16, kind="Internal").ap()
    wqdT_d = nc.dram_tensor("wqdT", [DI, C], BF16, kind="Internal").ap()

    dma_n = [0]
    out_n = [0]

    uid = [0]

    def uname(n):
        uid[0] += 1
        return f"{n}_{uid[0]}"

    st = contextlib.ExitStack()
    with st:
        dma_sem = st.enter_context(nc.semaphore("dma"))
        dma_gsem = st.enter_context(nc.semaphore("dmag"))
        dma_gn = [0]
        out_sem = st.enter_context(nc.semaphore("outs"))

        def sbt(name, shape, dt):
            return st.enter_context(nc.sbuf_tensor(uname(name), shape, dt))

        ident = sbt("identity", [128, 128], BF16)
        cvec = sbt("cvec", [128, CB], F32)
        euvec = sbt("euvec", [128, CB], F32)
        maskv = sbt("maskv_sb", [128, 1], F32)
        consts = sbt("consts", [128, 8], F32)
        carA = sbt("carA", [128, CB], F32)
        carB = sbt("carB", [128, CB], F32)
        preA = sbt("preA", [128, CB], F32)
        preB = sbt("preB", [128, CB], F32)
        rwkv = sbt("rwkv", [128, CB, MCH], BF16)
        stat = sbt("stat", [128, 32], F32)

        ps1 = st.enter_context(nc.psum_tensor("ps1", [128, 1024], F32))
        ps2 = st.enter_context(nc.psum_tensor("ps2", [128, 1024], F32))
        ps3 = st.enter_context(nc.psum_tensor("ps3", [128, 1024], F32))
        psT = st.enter_context(nc.psum_tensor("psT", [128, 1024], BF16))
        psTb = psT[:]

        def tap(name, src_ap):
            if not dbg:
                return
            with nc.Block() as blk:
                @blk.sync
                def _(sync):
                    sync.dma_start(out=dbg_d[name], in_=src_ap).then_inc(dma_sem, 16)
                    dma_n[0] += 16
                    sync.wait_ge(dma_sem, dma_n[0])

        def dma(q, out, in_, wait=True, gq=False):
            sem, cnt = (dma_gsem, dma_gn) if gq else (dma_sem, dma_n)
            q.dma_start(out=out, in_=in_).then_inc(sem, 16)
            cnt[0] += 16
            if wait:
                q.wait_ge(sem, cnt[0])
            return cnt[0]

        def wait_block(cnt):
            with nc.Block() as blk:
                @blk.sync
                def _(sync):
                    sync.wait_ge(dma_sem, cnt)

        def quant_blocks(src_ap, F, out_ap):
            G = F // 16
            src3 = src_ap.rearrange("p (g k) -> p g k", k=16)
            with nc.Block() as blk:
                @blk.vector
                def _(v):
                    v.tensor_reduce(qgm[:, :G], src3, AX, OP.max, apply_absolute_value=True)
                    v.tensor_scalar(qsc[:, :G], qgm[:, :G], 1.0 / 6.0, 1e-12, OP.mult, OP.max)
                    v.reciprocal(qrs[:, :G], qsc[:, :G])
                    v.tensor_copy(qscb[:, :G], qsc[:, :G])
                @blk.scalar
                def _(s):
                    s.activation(qsg[:, :F], src_ap, AF.Sign)
                    s.activation(qa[:, :F], src_ap, AF.Abs)
            with nc.Block() as blk:
                @blk.vector
                def _(v):
                    rsb = qrs[:, :G].unsqueeze(2).broadcast_to([128, G, 16])
                    v.tensor_tensor(qa[:, :F].rearrange("p (g k) -> p g k", k=16),
                                    qa[:, :F].rearrange("p (g k) -> p g k", k=16),
                                    rsb, OP.mult)
            with nc.Block() as blk:
                @blk.scalar
                def _(s):
                    s.activation(qt1[:, :F], qa[:, :F], AF.Identity, bias=consts[:, 0:1], scale=0.5)
                    s.activation(qt2[:, :F], qa[:, :F], AF.Identity, bias=consts[:, 1:2], scale=0.25)
            with nc.Block() as blk:
                @blk.vector
                def _(v):
                    v.tensor_tensor(qt1[:, :F], qt1[:, :F], qt2[:, :F], OP.min)
                    v.tensor_tensor(qt1[:, :F], qa[:, :F], qt1[:, :F], OP.min)
                    v.tensor_scalar(qiz[:, :F], qt1[:, :F], 2.0, None, OP.mult)
                    v.tensor_scalar(qrb[:, :F], qiz[:, :F], 0.5, None, OP.mult)
            with nc.Block() as blk:
                @blk.scalar
                def _(s):
                    s.activation(qr3[:, :F], qrb[:, :F], AF.Relu, bias=consts[:, 2:3])
                    s.activation(qr4[:, :F], qrb[:, :F], AF.Relu, bias=consts[:, 3:4])
            with nc.Block() as blk:
                @blk.vector
                def _(v):
                    v.scalar_tensor_tensor(qr4[:, :F], qr4[:, :F], 2.0, qr3[:, :F], OP.mult, OP.add)
                    v.tensor_tensor(qrb[:, :F], qr4[:, :F], qrb[:, :F], OP.add)
                    v.tensor_tensor(qrb[:, :F], qrb[:, :F], qsg[:, :F], OP.mult)
                    scb = qscb[:, :G].unsqueeze(2).broadcast_to([128, G, 16])
                    v.tensor_tensor(out_ap.rearrange("p (g k) -> p g k", k=16),
                                    qrb[:, :F].rearrange("p (g k) -> p g k", k=16),
                                    scb, OP.mult)

        def transpose_evac(src_bf_ap, ntile, dst3_ap):
            for base in range(0, ntile, 8):
                nb = min(8, ntile - base)
                with nc.Block() as blk:
                    @blk.tensor
                    def _(t, base=base, nb=nb):
                        for k in range(nb):
                            t.transpose(psTb[:, k * 128:(k + 1) * 128],
                                        src_bf_ap[:, (base + k) * 128:(base + k + 1) * 128],
                                        ident[:])
                with nc.Block() as blk:
                    @blk.scalar
                    def _(s, base=base, nb=nb):
                        s.copy(dst3_ap[:, base:base + nb, :],
                               psTb[:, :nb * 128].rearrange("p (a q) -> p a q", q=128))

        esem = {n: st.enter_context(nc.semaphore(uname("e_" + n)))
                for n in ("pe", "v", "s", "g", "sy")}
        ecnt = {n: 0 for n in esem}
        slot_sems = {}

        def slot_sem(name):
            if name not in slot_sems:
                slot_sems[name] = [st.enter_context(nc.semaphore(uname("ds_" + name))), 0]
            return slot_sems[name]

        class SB:
            EMAP = {"tensor": "pe", "vector": "v", "scalar": "s",
                    "gpsimd": "g", "sync": "sy"}

            def __init__(self):
                self.ops = {k: [] for k in self.EMAP}

            def add(self, eng, fn, waits=(), inc=True, dma=False):
                tok = None
                if dma:
                    assert isinstance(dma, str), "dma= must name a slot"
                    ss = slot_sem(dma)
                    ss[1] += 16
                    tok = (ss[0], ss[1])
                elif inc:
                    nm = self.EMAP[eng]
                    ecnt[nm] += 1
                    tok = (esem[nm], ecnt[nm])
                self.ops[eng].append((tuple(waits), fn, tok, dma))
                return tok

            def emit(self):
                with nc.Block() as blk:
                    def runner(entries):
                        def _(engobj):
                            for waits, fn, tok, isdma in entries:
                                for (sh, val) in waits:
                                    engobj.wait_ge(sh, val)
                                last = fn(engobj)
                                if tok is not None and last is not None:
                                    if isdma:
                                        last.then_inc(tok[0], 16)
                                    else:
                                        last.then_inc(tok[0], 1)
                        return _
                    for eng in ("sync", "tensor", "scalar", "vector", "gpsimd"):
                        if self.ops[eng]:
                            getattr(blk, eng)(runner(self.ops[eng]))

        def q_stream(B, src_ap, F, sc_, waits_v=(), waits_s=(), t1_extra=(), nsplit=1):
            assert F % (16 * nsplit) == 0
            Fh = F // nsplit
            toks = {}
            rng = [(h * Fh, Fh) for h in range(nsplit)]
            for h, (o, L) in enumerate(rng):
                G = L // 16
                go = o // 16
                sl = src_ap[:, o:o + L]
                s3 = sl.rearrange("p (g k) -> p g k", k=16)
                v1 = B.add("vector", lambda v, s3=s3, go=go, G=G: v.tensor_reduce(
                    sc_["qgm"][:, go:go + G], s3, AX, OP.max, apply_absolute_value=True),
                    waits=list(waits_v) if h == 0 else [])
                v2 = B.add("vector", lambda v, go=go, G=G: v.tensor_scalar(
                    sc_["qsc"][:, go:go + G], sc_["qgm"][:, go:go + G], 1.0 / 6.0, 1e-12,
                    OP.mult, OP.max), waits=[v1])
                v3 = B.add("vector", lambda v, go=go, G=G: v.reciprocal(
                    sc_["qrs"][:, go:go + G], sc_["qsc"][:, go:go + G]), waits=[v2])
                v4 = B.add("vector", lambda v, go=go, G=G: v.tensor_copy(
                    sc_["qscb"][:, go:go + G], sc_["qsc"][:, go:go + G]), waits=[v3])
                sg = B.add("scalar", lambda s, sl=sl, o=o, L=L: s.activation(
                    sc_["qsg"][:, o:o + L], sl, AF.Sign), waits=list(waits_s) if h == 0 else [])
                ab = B.add("scalar", lambda s, sl=sl, o=o, L=L: s.activation(
                    sc_["qa"][:, o:o + L], sl, AF.Abs), waits=[])
                toks[h] = dict(v3=v3, v4=v4, sg=sg, ab=ab)
            s1l = None
            for h, (o, L) in enumerate(rng):
                G = L // 16
                go = o // 16
                t = toks[h]
                am = B.add("vector", lambda v, o=o, L=L, go=go, G=G: v.tensor_tensor(
                    sc_["qa"][:, o:o + L].rearrange("p (g k) -> p g k", k=16),
                    sc_["qa"][:, o:o + L].rearrange("p (g k) -> p g k", k=16),
                    sc_["qrs"][:, go:go + G].unsqueeze(2).broadcast_to([128, G, 16]), OP.mult),
                    waits=[t["ab"], t["v3"]])
                s1 = B.add("scalar", lambda s, o=o, L=L: s.activation(
                    sc_["qt1"][:, o:o + L], sc_["qa"][:, o:o + L], AF.Identity,
                    bias=consts[:, 0:1], scale=0.5), waits=[am] + (list(t1_extra) if h == 0 else []))
                s2 = B.add("scalar", lambda s, o=o, L=L: s.activation(
                    sc_["qt2"][:, o:o + L], sc_["qa"][:, o:o + L], AF.Identity,
                    bias=consts[:, 1:2], scale=0.25), waits=[])
                m1 = B.add("vector", lambda v, o=o, L=L: v.tensor_tensor(
                    sc_["qt1"][:, o:o + L], sc_["qt1"][:, o:o + L], sc_["qt2"][:, o:o + L],
                    OP.min), waits=[s2])
                m2 = B.add("vector", lambda v, o=o, L=L: v.tensor_tensor(
                    sc_["qt1"][:, o:o + L], sc_["qa"][:, o:o + L], sc_["qt1"][:, o:o + L],
                    OP.min), waits=[])
                i1 = B.add("vector", lambda v, o=o, L=L: v.tensor_scalar(
                    sc_["qiz"][:, o:o + L], sc_["qt1"][:, o:o + L], 2.0, None, OP.mult), waits=[])
                r1 = B.add("vector", lambda v, o=o, L=L: v.tensor_scalar(
                    sc_["qrb"][:, o:o + L], sc_["qiz"][:, o:o + L], 0.5, None, OP.mult), waits=[])
                r3 = B.add("scalar", lambda s, o=o, L=L: s.activation(
                    sc_["qr3"][:, o:o + L], sc_["qrb"][:, o:o + L], AF.Relu,
                    bias=consts[:, 2:3]), waits=[r1])
                r4 = B.add("scalar", lambda s, o=o, L=L: s.activation(
                    sc_["qr4"][:, o:o + L], sc_["qrb"][:, o:o + L], AF.Relu,
                    bias=consts[:, 3:4]), waits=[])
                toks[h].update(r4=r4, i1=i1)
                s1l = s1
            out_tok = None
            for h, (o, L) in enumerate(rng):
                G = L // 16
                go = o // 16
                t = toks[h]
                u1 = B.add("vector", lambda v, o=o, L=L: v.scalar_tensor_tensor(
                    sc_["qr4"][:, o:o + L], sc_["qr4"][:, o:o + L], 2.0,
                    sc_["qr3"][:, o:o + L], OP.mult, OP.add), waits=[t["r4"]])
                u2 = B.add("vector", lambda v, o=o, L=L: v.tensor_tensor(
                    sc_["qrb"][:, o:o + L], sc_["qr4"][:, o:o + L], sc_["qrb"][:, o:o + L],
                    OP.add), waits=[])
                u3 = B.add("vector", lambda v, o=o, L=L: v.tensor_tensor(
                    sc_["qrb"][:, o:o + L], sc_["qrb"][:, o:o + L], sc_["qsg"][:, o:o + L],
                    OP.mult), waits=[t["sg"]])
                out_tok = B.add("vector", lambda v, o=o, L=L, go=go, G=G: v.tensor_tensor(
                    sc_["qout"][:, o:o + L].rearrange("p (g k) -> p g k", k=16),
                    sc_["qrb"][:, o:o + L].rearrange("p (g k) -> p g k", k=16),
                    sc_["qscb"][:, go:go + G].unsqueeze(2).broadcast_to([128, G, 16]), OP.mult),
                    waits=[t["v4"]])
            return out_tok, toks[nsplit - 1]["i1"], s1l

        def ln_stream(B, x_ap, out_ap, dump_ap, waits=(), so=0):
            c = lambda i: stat[:, so + i:so + i + 1]
            vm = B.add("vector", lambda v: v.tensor_reduce(c(0), x_ap, AX, OP.add),
                       waits=list(waits))
            sq = B.add("scalar", lambda s: s.activation(dump_ap, x_ap, AF.Square,
                                                        accum_out=c(1)),
                       waits=list(waits))
            t1 = B.add("vector", lambda v: v.tensor_scalar(c(2), c(0), 1.0 / C, None, OP.mult),
                       waits=[vm])
            t2 = B.add("vector", lambda v: v.tensor_tensor(c(3), c(2), c(2), OP.mult), waits=[t1])
            t3 = B.add("vector", lambda v: v.tensor_scalar(c(4), c(1), 1.0 / C, EPS,
                                                           OP.mult, OP.add), waits=[sq])
            t4 = B.add("vector", lambda v: v.tensor_tensor(c(5), c(4), c(3), OP.subtract),
                       waits=[t2, t3])
            s1 = B.add("scalar", lambda s: s.activation(c(6), c(5), AF.Sqrt), waits=[t4])
            t5 = B.add("vector", lambda v: v.reciprocal(c(7), c(6)), waits=[s1])
            t6 = B.add("vector", lambda v: v.tensor_tensor(c(3), c(2), c(7), OP.mult), waits=[t5])
            t7 = B.add("vector", lambda v: v.tensor_scalar(c(5), c(3), -1.0, None, OP.mult),
                       waits=[t6])
            return B.add("scalar", lambda s: s.activation(out_ap, x_ap, AF.Identity,
                                                          bias=c(5), scale=c(7)),
                         waits=[t7])

        def ln_blocks(x_ap, out_ap, dump_ap):
            with nc.Block() as blk:
                @blk.vector
                def _(v):
                    v.tensor_reduce(stat[:, 8:9], x_ap, AX, OP.add)
            with nc.Block() as blk:
                @blk.vector
                def _(v):
                    v.tensor_scalar(stat[:, 0:1], stat[:, 8:9], 1.0 / C, None, OP.mult)
            with nc.Block() as blk:
                @blk.scalar
                def _(s):
                    s.activation(dump_ap, x_ap, AF.Square,
                                 bias=stat[:, 0:1], scale=-1.0, accum_out=stat[:, 9:10])
            with nc.Block() as blk:
                @blk.vector
                def _(v):
                    v.tensor_scalar(stat[:, 1:2], stat[:, 9:10], 1.0 / C, EPS, OP.mult, OP.add)
            with nc.Block() as blk:
                @blk.scalar
                def _(s):
                    s.activation(stat[:, 2:3], stat[:, 1:2], AF.Sqrt)
            with nc.Block() as blk:
                @blk.vector
                def _(v):
                    v.reciprocal(stat[:, 3:4], stat[:, 2:3])
            with nc.Block() as blk:
                @blk.vector
                def _(v):
                    v.tensor_tensor(stat[:, 10:11], stat[:, 0:1], stat[:, 3:4], OP.mult)
            with nc.Block() as blk:
                @blk.vector
                def _(v):
                    v.tensor_scalar(stat[:, 4:5], stat[:, 10:11], -1.0, None, OP.mult)
            with nc.Block() as blk:
                @blk.scalar
                def _(s):
                    s.activation(out_ap, x_ap, AF.Identity,
                                 bias=stat[:, 4:5], scale=stat[:, 3:4])

        with nc.Block() as blk:
            @blk.sync
            def _(sync):
                dma(sync, ident[:], id_in, wait=False)
                dma(sync, maskv[:], mask_in, wait=False)
                dma(sync, cvec[:], td_in, wait=False)
                dma(sync, euvec[:], tf_in, wait=True)

            @blk.gpsimd
            def _(g):
                g.memset(consts[:, 0:1], 1.0)
                g.memset(consts[:, 1:2], 2.0)
                g.memset(consts[:, 2:3], -2.0)
                g.memset(consts[:, 3:4], -3.0)
                g.memset(carA[:], 0.0)
                g.memset(carB[:], 0.0)
                g.memset(preA[:], 0.0)
                g.memset(preB[:], 0.0)
        with nc.Block() as blk:
            @blk.scalar
            def _(s):
                s.activation(stat[:, 0:CB], cvec[:], AF.Exp)
        with nc.Block() as blk:
            @blk.scalar
            def _(s):
                s.activation(cvec[:], stat[:, 0:CB], AF.Exp, scale=-1.0)
        with nc.Block() as blk:
            @blk.scalar
            def _(s):
                s.activation(euvec[:], euvec[:], AF.Exp)

        with nc.named_scope("wquant"), contextlib.ExitStack() as parena:
            pn = lambda n, sh, dt: parena.enter_context(nc.sbuf_tensor(uname(n), sh, dt))
            wstage = pn("wstage", [128, 2, DI], F32)
            wtr = pn("wtr", [128, 2, IB, 128], BF16)
            wsc = []
            for p in range(2):
                d = {"qa": pn("wqa", [128, DI], F32), "qt1": pn("wqt1", [128, DI], F32),
                     "qt2": pn("wqt2", [128, DI], F32), "qrb": pn("wqrb", [128, DI], BF16),
                     "qr3": pn("wqr3", [128, DI], BF16), "qr4": pn("wqr4", [128, DI], BF16),
                     "qsg": pn("wqsg", [128, DI], BF16), "qgm": pn("wqgm", [128, DI // 16], F32),
                     "qsc": pn("wqsc", [128, DI // 16], F32), "qrs": pn("wqrs", [128, DI // 16], F32),
                     "qscb": pn("wqscb", [128, DI // 16], BF16)}
                d["qiz"] = d["qt2"][:].bitcast(I32)
                d["qout"] = d["qa"][:].bitcast(BF16)[:, :DI]
                wsc.append(d)
            wtiles = ([(wg_in, wqgT_d, 2 * i, 2, 2 * C) for i in range(IB // 2)] +
                      [(wg_in, wqgT_d, IB - 1, 1, C)] +
                      [(wu_in, wquT_d, 2 * i, 2, 2 * C) for i in range(IB // 2)] +
                      [(wu_in, wquT_d, IB - 1, 1, C)] +
                      [(wd_in, wqdT_d, i, 1, DI) for i in range(CB)])
            B = SB()
            load_toks, sa_toks, tr_toks, ev_toks = {}, {}, {}, {}
            def wq_load(wi):
                srcw, dst, row0, nrows, F = wtiles[wi]
                par = wi % 2
                w = []
                if wi >= 2:
                    w = [sa_toks[wi - 2]]
                load_toks[wi] = B.add("sync", lambda sy, par=par, srcw=srcw, row0=row0, nrows=nrows, F=F:
                                      sy.dma_start(out=wstage[:, par, :F].rearrange("p (a c) -> p a c", a=nrows),
                                                   in_=srcw[row0 * 128:(row0 + nrows) * 128, :].rearrange(
                                                       "(a p) c -> p a c", p=128)),
                                      waits=w, dma=f"wst{par}")
            wq_load(0)
            wq_load(1)
            prev_ev = None
            for wi, (srcw, dst, row0, nrows, F) in enumerate(wtiles):
                par = wi % 2
                sc_ = wsc[par]
                ncols = F // 128
                vo, i1, s1 = q_stream(B, wstage[:, par, :F],
                                      F, sc_, waits_v=[load_toks[wi]], waits_s=[load_toks[wi]],
                                      t1_extra=(), nsplit=2)
                sa_toks[wi] = s1
                pe_w = [vo]
                if wi >= 2 and (wi - 2) in tr_toks:
                    pass
                evs = []
                for bi, base in enumerate(range(0, ncols, 8)):
                    nb = min(8, ncols - base)
                    tw = list(pe_w) if bi == 0 else []
                    if prev_ev is not None:
                        tw.append(prev_ev)
                    if bi == 0 and wi - 2 in ev_toks:
                        tw.append(ev_toks[wi - 2])
                    ptb = B.add("tensor", lambda t, sc_=sc_, base=base, nb=nb: (
                        [t.transpose(psTb[:, k * 128:(k + 1) * 128],
                                     sc_["qout"][:, (base + k) * 128:(base + k + 1) * 128], ident[:])
                         for k in range(nb)])[-1], waits=tw)
                    prev_ev = B.add("scalar", lambda s, par=par, base=base, nb=nb: s.copy(
                        wtr[:, par, base:base + nb, :],
                        psTb[:, :nb * 128].rearrange("p (a q) -> p a q", q=128)), waits=[ptb])
                    evs.append(prev_ev)
                if wi + 2 < len(wtiles):
                    wq_load(wi + 2)
                cpf = ncols // nrows
                st_w = [evs[-1]]
                for a in range(nrows):
                    ev_toks[wi] = B.add("sync", lambda sy, dst=dst, row0=row0, a=a, par=par, cpf=cpf: sy.dma_start(
                        out=dst.rearrange("(cb p) i -> p cb i", p=128)[:, :, (row0 + a) * 128:(row0 + a + 1) * 128],
                        in_=wtr[:, par, a * cpf:(a + 1) * cpf, :]), waits=st_w if a == 0 else [], dma=f"wqst{par}")
            def waitall(sy):
                for nm in ("wqst0", "wqst1"):
                    if nm in slot_sems:
                        sy.wait_ge(slot_sems[nm][0], slot_sems[nm][1])
                return None
            B.add("sync", waitall, inc=False)
            B.emit()

        qa = sbt("qa", [128, DI], F32)
        qt1 = sbt("qt1", [128, DI], F32)
        qt2 = sbt("qt2", [128, DI], F32)
        qiz = qt2[:].bitcast(I32)
        qrb = sbt("qrb", [128, DI], BF16)
        qr3 = sbt("qr3", [128, DI], BF16)
        qr4 = sbt("qr4", [128, DI], BF16)
        qsg = sbt("qsg", [128, DI], BF16)
        qout = qa[:].bitcast(BF16)[:, :DI]
        qgm = sbt("qgm", [128, DI // 16], F32)
        qsc = sbt("qsc", [128, DI // 16], F32)
        qrs = sbt("qrs", [128, DI // 16], F32)
        qscb = sbt("qscb", [128, DI // 16], BF16)

        def emit_mixer_chunk(ci, prefix, bufs, tap_this=False):
            (wrvk_str, xt_bf, ekb, vvb, ssb, Abuf, Bbuf, numb, ctile, thb) = bufs
            xsrc = xT_pre if prefix else xT_own
            t0 = ci * MCH
            B = SB()
            dxt = B.add("gpsimd", lambda g: g.dma_start(
                out=xt_bf[:], in_=xsrc.rearrange("(cb p) t -> p cb t", p=128)[:, :, t0:t0 + MCH]),
                dma="xt")
            wload, pe_dt, sev, vch = {}, {}, {}, {}
            orders = {}
            for j in range(CB):
                orders[j] = ([(CB + j, ps1), (2 * CB + j, ps2)] if prefix
                             else [(j, ps3), (CB + j, ps1), (2 * CB + j, ps2)])
            def wrvk_load(j):
                par = j % 2
                w = [pe_dt[(j - 2, len(orders[j - 2]) - 1)]] if j >= 2 else []
                for oi, (dt, _ps) in enumerate(orders[j]):
                    wload[(j, oi)] = B.add("gpsimd", lambda g, par=par, oi=oi, dt=dt: g.dma_start(
                        out=wrvk_str[:, par, oi, :, :],
                        in_=wrvkT.rearrange("(cb p) d -> p cb d", p=128)[:, :, dt * 128:(dt + 1) * 128]),
                        waits=w if oi == 0 else [], dma=f"wrvk{par}")
                wload[j] = wload[(j, len(orders[j]) - 1)]
            wrvk_load(0)
            wrvk_load(1)
            for j in range(CB):
                par = j % 2
                order = orders[j]
                for oi, (dt, ps) in enumerate(order):
                    w = [wload[j]]
                    if oi == 0:
                        w.append(dxt)
                    if j >= 1 and (j - 1, oi) in sev:
                        w.append(sev[(j - 1, oi)])
                    def mmj(t, ps=ps, oi=oi, par=par):
                        for tt in range(n4):
                            for cb in range(CB):
                                last = t.matmul(ps[:, tt * 512:(tt + 1) * 512],
                                                wrvk_str[:, par, oi, cb, :],
                                                xt_bf[:, cb, tt * 512:(tt + 1) * 512],
                                                start=(cb == 0), stop=(cb == CB - 1))
                        return last
                    pe_dt[(j, oi)] = B.add("tensor", mmj, waits=w)
                if j + 2 < CB:
                    wrvk_load(j + 2)
                evw2 = [vch[j - 2]] if j >= 2 else []
                if prefix:
                    sev[(j, 0)] = B.add("scalar", lambda s, par=par: s.copy(vvb[:, par, :], ps1[:, :MCH]),
                                        waits=[pe_dt[(j, 0)]] + evw2)
                    sev[(j, 1)] = B.add("scalar", lambda s, par=par: s.activation(
                        ekb[:, par, :], ps2[:, :MCH], AF.Exp), waits=[pe_dt[(j, 1)]])
                else:
                    sev[(j, 0)] = B.add("scalar", lambda s, par=par: s.activation(
                        thb[:, par, :], ps3[:, :MCH], AF.Tanh, scale=0.5),
                        waits=[pe_dt[(j, 0)]] + evw2)
                    sev[(j, 1)] = B.add("scalar", lambda s, par=par: s.copy(vvb[:, par, :], ps1[:, :MCH]),
                                        waits=[pe_dt[(j, 1)]])
                    sev[(j, 2)] = B.add("scalar", lambda s, par=par: s.activation(
                        ekb[:, par, :], ps2[:, :MCH], AF.Exp), waits=[pe_dt[(j, 2)]])
                def chain(v, j=j, par=par, prefix=prefix):
                    v.tensor_scalar(ctile[:], ekb[:, par, :], 0.0, cvec[:, j:j + 1], OP.mult, OP.add)
                    v.tensor_tensor(ssb[:], ekb[:, par, :], vvb[:, par, :], OP.mult)
                    if prefix:
                        v.tensor_tensor_scan(Abuf[:, :MCH], ctile[:], ssb[:],
                                             preA[:, j:j + 1], OP.mult, OP.add)
                        v.tensor_tensor_scan(Bbuf[:, :MCH], ctile[:], ekb[:, par, :],
                                             preB[:, j:j + 1], OP.mult, OP.add)
                        v.tensor_copy(preA[:, j:j + 1], Abuf[:, MCH - 1:MCH])
                        return v.tensor_copy(preB[:, j:j + 1], Bbuf[:, MCH - 1:MCH])
                    v.tensor_copy(Abuf[:, 0:1], carA[:, j:j + 1])
                    v.tensor_copy(Bbuf[:, 0:1], carB[:, j:j + 1])
                    v.tensor_tensor_scan(Abuf[:, 1:MCH + 1], ctile[:], ssb[:],
                                         carA[:, j:j + 1], OP.mult, OP.add)
                    v.tensor_tensor_scan(Bbuf[:, 1:MCH + 1], ctile[:], ekb[:, par, :],
                                         carB[:, j:j + 1], OP.mult, OP.add)
                    v.tensor_copy(carA[:, j:j + 1], Abuf[:, MCH:MCH + 1])
                    v.tensor_copy(carB[:, j:j + 1], Bbuf[:, MCH:MCH + 1])
                    v.scalar_tensor_tensor(numb[:], ssb[:], euvec[:, j:j + 1],
                                           Abuf[:, 0:MCH], OP.mult, OP.add)
                    return v.scalar_tensor_tensor(ssb[:], ekb[:, par, :], euvec[:, j:j + 1],
                                                  Bbuf[:, 0:MCH], OP.mult, OP.add)
                vch1 = B.add("vector", chain, waits=[sev[(j, len(order) - 1)]])
                if prefix:
                    vch[j] = vch1
                else:
                    srd = B.add("scalar", lambda s, par=par: (
                        s.activation(ctile[:], ssb[:], AF.Ln),
                        s.activation(Bbuf[:, 0:MCH], ctile[:], AF.Exp, scale=-1.0))[-1],
                        waits=[vch1])
                    def chain2(v, j=j, par=par):
                        v.tensor_tensor(numb[:], numb[:], Bbuf[:, 0:MCH], OP.mult)
                        v.tensor_scalar(thb[:, par, :], thb[:, par, :], 0.5, 0.5, OP.mult, OP.add)
                        return v.tensor_tensor(rwkv[:, j, :], numb[:], thb[:, par, :], OP.mult)
                    vch[j] = B.add("vector", chain2, waits=[srd])
            B.emit()

        def emit_post_sub(ci, sc, bufs, tap_this=False):
            (wout_bf, x1n, x1t, xst, xqT, wgu, hbuf, hqT, wdT, x2t, outst) = bufs
            assert STT == 2
            t0 = ci * MCH
            B = SB()
            x1ta, x1tb = qt1[:, 0:C], qt1[:, C:2 * C]
            psc = {"qa": qa[:], "qt1": qt1[:], "qt2": qt2[:], "qiz": qiz, "qrb": qrb[:],
                   "qr3": qr3[:], "qr4": qr4[:], "qsg": qsg[:], "qgm": qgm[:],
                   "qsc": qsc[:], "qrs": qrs[:], "qscb": qscb[:], "qout": qout}

            dxx = None
            for tt in range(STT):
                dxx = B.add("sync", lambda sy, tt=tt, gt=t0 + (sc * STT + tt) * 128:
                            sy.dma_start(out=xst[:, tt, :], in_=x_own[gt:gt + 128, :]), dma="x")
            dwd0 = B.add("sync", lambda sy: sy.dma_start(
                out=wdT[:], in_=wqdT_d.rearrange("(ib p) d -> p ib d", p=128)[:, :, 0:512]), dma="wd0")
            dgu = {}
            def gu_load(par, ni, waits=()):
                o, n = sum(NQ[:ni]), NQ[ni]
                B.add("sync", lambda sy, par=par, o=o, n=n: sy.dma_start(
                    out=wgu[:, par, 0, :, :n],
                    in_=wqgT_d.rearrange("(cb p) i -> p cb i", p=128)[:, :, o:o + n]),
                    waits=waits, dma=f"gu{par}")
                dgu[ni] = B.add("sync", lambda sy, par=par, o=o, n=n: sy.dma_start(
                    out=wgu[:, par, 1, :, :n],
                    in_=wquT_d.rearrange("(cb p) i -> p cb i", p=128)[:, :, o:o + n]),
                    dma=f"gu{par}")
            gu_load(0, 0)
            gu_load(1, 1)

            def mm2(t, lt, ps):
                for half in range(2):
                    for j in range(CB):
                        last = t.matmul(ps[:, half * 512:(half + 1) * 512],
                                        rwkv[:, j, lt * 128:(lt + 1) * 128],
                                        wout_bf[:, j, half * 512:(half + 1) * 512],
                                        start=(j == 0), stop=(j == CB - 1))
                return last
            pmm2a = B.add("tensor", lambda t: mm2(t, sc * STT + 0, ps1), waits=[dxx])
            pmm2b = B.add("tensor", lambda t: mm2(t, sc * STT + 1, ps2), waits=[])
            vra = B.add("vector", lambda v: v.tensor_tensor(x1ta, ps1[:, :C], xst[:, 0, :], OP.add),
                        waits=[pmm2a])
            vrb = B.add("vector", lambda v: v.tensor_tensor(x1tb, ps2[:, :C], xst[:, 1, :], OP.add),
                        waits=[pmm2b])
            ln0 = ln_stream(B, x1ta, x1n[:, 0, :], qt2[:, 0:C], waits=[vra], so=16)
            ln1t = ln_stream(B, x1tb, x1n[:, 1, :], qt2[:, C:2 * C], waits=[vrb], so=24)

            xf = x1n[:].rearrange("p a c -> p (a c)")
            xqv, xq_i1, xq_s1 = q_stream(B, xf, 2 * C, psc, waits_v=[ln1t], waits_s=[ln1t],
                                         t1_extra=[], nsplit=2)
            def trx(t, base):
                for k in range(8):
                    last = t.transpose(psTb[:, k * 128:(k + 1) * 128],
                                       qout[:, (base + k) * 128:(base + k + 1) * 128], ident[:])
                return last
            ptr0 = B.add("tensor", lambda t: trx(t, 0), waits=[xqv])
            se0 = B.add("scalar", lambda s: s.copy(xqT[:, :, 0:128],
                                                   psTb[:, :1024].rearrange("p (a q) -> p a q", q=128)),
                        waits=[ptr0])
            ptr1 = B.add("tensor", lambda t: trx(t, 8), waits=[se0])
            se1 = B.add("scalar", lambda s: s.copy(xqT[:, :, 128:256],
                                                   psTb[:, :1024].rearrange("p (a q) -> p a q", q=128)),
                        waits=[ptr1])

            hv = None
            sil = {}
            off = 0
            for ni, nw in enumerate(NQ):
                par = ni % 2
                gps = ps1 if par == 0 else ps3
                gw = [dgu[ni], se1]
                if ni >= 2:
                    gw.append(sil[ni - 2])
                if ni == 0:
                    gw += [vra, vrb]
                def gmm(t, gps=gps, par=par, nw=nw):
                    for tt in range(STT):
                        for cb in range(CB):
                            last = t.matmul(gps[:, tt * 512:tt * 512 + nw],
                                            xqT[:, cb, tt * 128:(tt + 1) * 128],
                                            wgu[:, par, 0, cb, :nw],
                                            start=(cb == 0), stop=(cb == CB - 1))
                    return last
                pg = B.add("tensor", gmm, waits=gw)
                sl = B.add("scalar", lambda s, gps=gps, nw=nw: (
                    s.activation(qt1[:, :nw], gps[:, :nw], AF.Silu),
                    s.activation(qt1[:, 512:512 + nw], gps[:, 512:512 + nw], AF.Silu))[-1],
                    waits=[pg, xq_i1] if ni == 0 else [pg])
                sil[ni] = sl
                uw = [sl]
                if hv is not None:
                    uw.append(hv)
                def umm(t, par=par, nw=nw):
                    for tt in range(STT):
                        for cb in range(CB):
                            last = t.matmul(ps2[:, tt * 512:tt * 512 + nw],
                                            xqT[:, cb, tt * 128:(tt + 1) * 128],
                                            wgu[:, par, 1, cb, :nw],
                                            start=(cb == 0), stop=(cb == CB - 1))
                    return last
                pu = B.add("tensor", umm, waits=uw)
                hv = B.add("vector", lambda v, off=off, nw=nw: (
                    v.tensor_tensor(hbuf[:, 0, off:off + nw], qt1[:, :nw], ps2[:, :nw], OP.mult),
                    v.tensor_tensor(hbuf[:, 1, off:off + nw], qt1[:, 512:512 + nw],
                                    ps2[:, 512:512 + nw], OP.mult))[-1],
                    waits=[pu])
                if ni + 2 < len(NQ):
                    gu_load(par, ni + 2, waits=[pu])
                off += nw
            wdT1 = wgu[:].rearrange("p a b cb i -> p (a b cb i)")[:, :IB * 512].rearrange(
                "p (ib i) -> p ib i", i=512)
            dwd1 = B.add("sync", lambda sy: sy.dma_start(
                out=wdT1, in_=wqdT_d.rearrange("(ib p) d -> p ib d", p=128)[:, :, 512:1024]),
                waits=[hv], dma="wd1")

            prev_evac = se1
            hq_toks = {}
            def tail(tt):
                psx = ps1 if tt == 0 else ps2
                vr2 = B.add("vector", lambda v, psx=psx, tt=tt: v.tensor_tensor(
                    xst[:, tt, :], psx[:, :C], x1n[:, tt, :], OP.add),
                    waits=[hq_toks[(tt, "pm3")], hq_toks[(tt, "s1")]])
                nmo = ln_stream(B, xst[:, tt, :], outst[:, tt % 2, :], x1n[:, tt, :],
                                waits=[vr2, (out_sem, max(0, out_n[0] - 16))],
                                so=16 if tt == 0 else 24)
                gt = t0 + (sc * STT + tt) * 128
                B.add("gpsimd", lambda g, gt=gt, tt=tt: g.dma_start(
                    out=out_d[gt:gt + 128, :], in_=outst[:, tt % 2, :]), waits=[nmo],
                    inc=False, dma=False)
                ent = B.ops["gpsimd"][-1]
                def mk(fn0):
                    def _f(g):
                        last = fn0(g)
                        last.then_inc(out_sem, 16)
                        return None
                    return _f
                B.ops["gpsimd"][-1] = (ent[0], mk(ent[1]), None, False)
                out_n[0] += 16

            prev_i1 = None
            for tt in range(STT):
                ha = hbuf[:, tt, :]
                w0 = [hv]
                hqv, h_i1, h_s1 = q_stream(B, ha, DI, psc,
                                           waits_v=w0, waits_s=w0,
                                           t1_extra=[xq_i1] if tt == 0 else [prev_i1],
                                           nsplit=2)
                prev_i1 = h_i1
                hq_toks[(tt, "s1")] = h_s1
                evs = []
                for base in range(0, IB, 8):
                    nb = min(8, IB - base)
                    ptb = B.add("tensor", lambda t, base=base, nb=nb: (
                        [t.transpose(psTb[:, k * 128:(k + 1) * 128],
                                     qout[:, (base + k) * 128:(base + k + 1) * 128], ident[:])
                         for k in range(nb)])[-1], waits=[hqv, prev_evac])
                    prev_evac = B.add("scalar", lambda s, tt=tt, base=base, nb=nb: s.copy(
                        hqT[:, tt, base:base + nb, :],
                        psTb[:, :nb * 128].rearrange("p (a q) -> p a q", q=128)), waits=[ptb])
                    evs.append(prev_evac)
                psx = ps1 if tt == 0 else ps2
                mmw = [evs[-1], dwd0, dwd1]
                if tt == 0:
                    mmw += [hv, sil[len(NQ) - 2]]
                hq_toks[(tt, "pm3")] = B.add("tensor", lambda t, psx=psx, tt=tt: (
                    [t.matmul(psx[:, half * 512:(half + 1) * 512], hqT[:, tt, ib, :],
                              (wdT if half == 0 else wdT1)[:, ib, :],
                              start=(ib == 0), stop=(ib == IB - 1))
                     for half in range(2) for ib in range(IB)][-1]), waits=mmw)
                if tt == 1:
                    tail(0)
            tail(1)
            B.emit()

        def with_mixer_arena(fn):
            with contextlib.ExitStack() as ma:
                mb = lambda name, shape, dt: ma.enter_context(nc.sbuf_tensor(uname(name), shape, dt))
                bufs = (mb("wrvk_str", [128, 2, 3, CB, 128], BF16),
                        mb("xt_bf", [128, CB, MCH], BF16),
                        mb("ekb", [128, 2, MCH], F32),
                        mb("vvb", [128, 2, MCH], F32),
                        mb("ssb", [128, MCH], F32),
                        mb("Abuf", [128, MCH + 1], F32),
                        mb("Bbuf", [128, MCH + 1], F32),
                        mb("numb", [128, MCH], F32),
                        mb("ctile", [128, MCH], F32),
                        mb("thb", [128, 2, MCH], F32))
                fn(bufs)

        with nc.named_scope("prefix"):
            for ci in range(NMC):
                with_mixer_arena(lambda bufs, ci=ci: emit_mixer_chunk(ci, True, bufs))
        with nc.Block() as blk:
            @blk.vector
            def _(v):
                v.tensor_scalar(carA[:], preA[:], maskv[:, 0:1], None, OP.mult)
                v.tensor_scalar(carB[:], preB[:], maskv[:, 0:1], None, OP.mult)
        tap("d_carA", carA[:])

        for ci in range(NMC):
            with nc.named_scope(f"mixer{ci}"):
                with_mixer_arena(lambda bufs, ci=ci: emit_mixer_chunk(ci, False, bufs, tap_this=(ci == 0)))
            if dbg and ci == 0:
                tap("d_rwkv", rwkv[:])
            with contextlib.ExitStack() as fa:
                fb = lambda name, shape, dt: fa.enter_context(nc.sbuf_tensor(uname(name), shape, dt))
                wout_bf = fb("wout_bf", [128, CB, C], BF16)
                x1n = fb("x1n", [128, STT, C], F32)
                x1t = qt1[:, 0:C]
                xst = fb("xst", [128, STT, C], F32)
                xqT = fb("xqT", [128, CB, SUB], BF16)
                wgu = fb("wgu", [128, 2, 2, CB, 512], BF16)
                hbuf = fb("hbuf", [128, STT, DI], F32)
                hqT = fb("hqT", [128, 2, IB, 128], BF16)
                wdT = fb("wdT", [128, IB, 512], BF16)
                x2t = None
                outst = fb("outst", [128, 2, C], F32)
                with nc.Block() as blk:
                    @blk.gpsimd
                    def _(g):
                        dma(g, wout_bf[:],
                            woutT.rearrange("(cb p) d -> p cb d", p=128), gq=True)
                fbufs = (wout_bf, x1n, x1t, xst, xqT, wgu, hbuf, hqT, wdT, x2t, outst)
                for sc in range(NSUB):
                    with nc.named_scope(f"post{ci}_{sc}"):
                        emit_post_sub(ci, sc, fbufs, tap_this=(ci == 0 and sc == 0))

        with nc.Block() as blk:
            @blk.gpsimd
            def _(g):
                g.wait_ge(out_sem, out_n[0])
    return nc


def kernel(L0_attn_x, L0_attn_W_rvk, L0_attn_W_out, L0_attn_time_decay,
           L0_attn_time_first, L0_ffn_w_gate, L0_ffn_w_up, L0_ffn_w_down):
    B, T, _ = L0_attn_x.shape
    TH = T // 2
    x = np.ascontiguousarray(np.asarray(L0_attn_x, np.float32))
    wrvkT = np.ascontiguousarray(np.asarray(L0_attn_W_rvk, np.float32).T)
    woutT = np.ascontiguousarray(np.asarray(L0_attn_W_out, np.float32).T)
    wg = np.ascontiguousarray(np.asarray(L0_ffn_w_gate, np.float32))
    wu = np.ascontiguousarray(np.asarray(L0_ffn_w_up, np.float32))
    wd = np.ascontiguousarray(np.asarray(L0_ffn_w_down, np.float32))
    td = np.ascontiguousarray(np.asarray(L0_attn_time_decay, np.float32))
    tf = np.ascontiguousarray(np.asarray(L0_attn_time_first, np.float32))
    ident = np.eye(128, dtype=ml_dtypes.bfloat16).view(np.uint16)

    nc = build_core_program(TH, MCH=1024, SUB=256)
    in_maps = []
    for c in range(8):
        b, h = c // 2, c % 2
        xb = x[b]
        xbT = np.ascontiguousarray(xb.T)
        own = slice(h * TH, (h + 1) * TH)
        in_maps.append({
            "xT_own": np.ascontiguousarray(xbT[:, own]),
            "xT_pre": np.ascontiguousarray(xbT[:, 0:TH]),
            "x_own": np.ascontiguousarray(xb[own]),
            "wrvkT": wrvkT, "woutT": woutT,
            "wg": wg, "wu": wu, "wd": wd,
            "td": np.ascontiguousarray(td.reshape(CB, 128).T),
            "tf": np.ascontiguousarray(tf.reshape(CB, 128).T),
            "maskv": np.full((128, 1), float(h), np.float32),
            "ident": ident,
        })
    res = run_bass_kernel_spmd(nc, in_maps, core_ids=list(range(8)))
    out = np.empty((B, T, C), np.float32)
    for c in range(8):
        b, h = c // 2, c % 2
        out[b, h * TH:(h + 1) * TH] = np.asarray(res.results[c]["out"])
    return out
